# version 01
# baseline (speedup 1.0000x reference)
import math
from contextlib import ExitStack
import numpy as np
import ml_dtypes
import concourse.bass as bass
import concourse.mybir as mybir
from concourse.bass_utils import run_bass_kernel_spmd

F32 = mybir.dt.float32
BF16 = mybir.dt.bfloat16
F16 = mybir.dt.float16
I32 = mybir.dt.int32
AF = mybir.ActivationFunctionType
ALU = mybir.AluOpType
BF = ml_dtypes.bfloat16

NCORE = 8
T = 2048
D = 4096
MD = 512
MF = 2048
NT = T // 128
EPS = 1e-5


def alibi_slopes(n_heads=32):
    cp2 = 2 ** math.floor(math.log2(n_heads))
    base = 2.0 ** (-(2.0 ** (-(math.log2(cp2) - 3))))
    slopes = [base ** k for k in range(1, cp2 + 1)]
    if cp2 != n_heads:
        eb = 2.0 ** (-(2.0 ** (-(math.log2(2 * cp2) - 3))))
        ne = min(cp2, n_heads - cp2)
        slopes += [eb ** k for k in range(1, 2 * ne + 1, 2)]
    return np.asarray(slopes, dtype=np.float32)


class S:

    def __init__(self, nc, name):
        self.h = nc.alloc_semaphore(name)
        self.n = 0

    def inc(self, inst, k=1):
        inst.then_inc(self.h, k)
        self.n += k
        return inst

    def wait(self, eng, val=None):
        eng.wait_ge(self.h, self.n if val is None else val)


def emit_ln(nc, st, block, x, xn, tag=""):
    sb = lambda n, sh, dt: st.enter_context(nc.sbuf_tensor(n + tag, sh, dt))
    xt = [sb(f"lnxt{i}", [128, D], F32) for i in range(2)]
    ot = [sb(f"lnot{i}", [128, D], BF16) for i in range(2)]
    scr = sb("lnscr", [128, D], F32)
    sums = sb("lnsums", [128, 2], F32)
    sumsq = sb("lnsumsq", [128, 2], F32)
    mu = sb("lnmu", [128, 2], F32)
    ve = sb("lnve", [128, 2], F32)
    rstd = sb("lnrstd", [128, 2], F32)
    nmub = sb("lnnmub", [128, 2], F32)
    ld = S(nc, "lnld" + tag)
    vs = S(nc, "lnvs" + tag)
    sd = S(nc, "lnsd" + tag)
    stq = S(nc, "lnst" + tag)
    va = S(nc, "lnva" + tag)

    @block.sync
    def _(sync, sem=None):
        for i in range(NT):
            if i >= 2:
                sync.wait_ge(va.h, i - 1)
            sync.dma_start(xt[i % 2][:, :], x[128 * i:128 * (i + 1), :]).then_inc(ld.h, 16)

    @block.vector
    def _(v, sem=None):
        for i in range(NT):
            j = i % 2
            v.wait_ge(ld.h, 16 * (i + 1))
            v.tensor_scalar(scr[:, :], xt[j][:, :], 1.0, 0.0, ALU.mult, ALU.add,
                            accum_out=sums[:, j:j + 1])
            v.scalar_tensor_tensor(scr[:, :], xt[j][:, :], 1.0, xt[j][:, :],
                                   ALU.mult, ALU.mult, accum_out=sumsq[:, j:j + 1])
            v.tensor_scalar(nmub[:, j:j + 1], sums[:, j:j + 1], -1.0 / D, None, ALU.mult)
            v.tensor_scalar(ve[:, j:j + 1], sumsq[:, j:j + 1], 1.0 / D, EPS,
                            ALU.mult, ALU.add)
            v.tensor_tensor(scr[:, 0:1], nmub[:, j:j + 1], nmub[:, j:j + 1], ALU.mult)
            v.tensor_tensor(ve[:, j:j + 1], ve[:, j:j + 1], scr[:, 0:1], ALU.subtract)
            v.reciprocal(ve[:, j:j + 1], ve[:, j:j + 1]).then_inc(vs.h, 1)
            v.wait_ge(sd.h, i + 1)
            v.tensor_scalar(ot[j][:, :], xt[j][:, :], nmub[:, j:j + 1],
                            rstd[:, j:j + 1], ALU.add, ALU.mult).then_inc(va.h, 1)

    @block.scalar
    def _(sc):
        for i in range(NT):
            j = i % 2
            sc.wait_ge(vs.h, i + 1)
            if i >= 2:
                sc.wait_ge(va.h, i - 1)
            sc.activation(rstd[:, j:j + 1], ve[:, j:j + 1], AF.Sqrt).then_inc(sd.h, 1)

    @block.gpsimd
    def _(g, sem=None):
        for i in range(NT):
            g.wait_ge(va.h, i + 1)
            g.dma_start(xn[128 * i:128 * (i + 1), :], ot[i % 2][:, :]).then_inc(stq.h, 16)
        g.wait_ge(stq.h, 16 * NT)


def build_ln(xname="x", oname="xn"):
    nc = bass.Bass("TRN2", target_bir_lowering=False)
    x = nc.dram_tensor(xname, [T, D], F32, kind="ExternalInput")
    xn = nc.dram_tensor(oname, [T, D], BF16, kind="ExternalOutput")
    with ExitStack() as st:
        block = st.enter_context(nc.Block())
        emit_ln(nc, st, block, x, xn)
    return nc


def build_h0():
    nc = bass.Bass("TRN2", target_bir_lowering=False)
    xnT = nc.dram_tensor("xnT", [D, T], BF16, kind="ExternalInput")
    WinT = nc.dram_tensor("WinT", [D, MD], F32, kind="ExternalInput")
    bin_ = nc.dram_tensor("bin", [1, MD], F32, kind="ExternalInput")
    h0 = nc.dram_tensor("h0", [T, MD], F32, kind="ExternalOutput")
    NV = D // 128
    with ExitStack() as st:
        xsb_ = [st.enter_context(nc.sbuf_tensor(f"xsb{v}", [128, T], BF16))
                for v in range(NV)]
        xsb = lambda v, lo, hi: xsb_[v][:, lo:hi]
        wsb = st.enter_context(nc.sbuf_tensor("wsb", [128, NV * MD], BF16))
        brep = st.enter_context(nc.sbuf_tensor("brep", [128, MD], F32))
        hout = [st.enter_context(nc.sbuf_tensor(f"hout{i}", [128, MD], F32)) for i in range(2)]
        ps = [st.enter_context(nc.psum_tensor(f"ps{i}", [128, MD], F32)) for i in range(2)]
        ld = S(nc, "ld")
        wld = S(nc, "wld")
        mm = S(nc, "mm")
        ev = S(nc, "ev")
        stq = S(nc, "stq")
        block = st.enter_context(nc.Block())

        @block.sync
        def _(sync, sem=None):
            sync.dma_start(brep[:, :], bass.AP(bin_, 0, [[0, 128], [1, MD]])).then_inc(ld.h, 16)
            for v in range(NV):
                sync.dma_start(xsb(v, 0, T),
                               xnT[128 * v:128 * (v + 1), :]).then_inc(ld.h, 16)

        @block.gpsimd
        def _(g, sem=None):
            for v in range(NV):
                g.dma_start(wsb[:, v * MD:(v + 1) * MD],
                            WinT[128 * v:128 * (v + 1), :]).then_inc(wld.h, 16)
            for t in range(NT):
                g.wait_ge(ev.h, t + 1)
                g.dma_start(h0[128 * t:128 * (t + 1), :], hout[t % 2][:, :]).then_inc(stq.h, 16)
            g.wait_ge(stq.h, 16 * NT)

        @block.tensor
        def _(te, sem=None):
            te.wait_ge(ld.h, 16 * (NV + 1))
            te.wait_ge(wld.h, 16 * NV)
            for t in range(NT):
                if t >= 2:
                    te.wait_ge(ev.h, t - 1)
                for v in range(NV):
                    mi = te.matmul(ps[t % 2][:, :],
                              xsb(v, 128 * t, 128 * (t + 1)),
                              wsb[:, v * MD:(v + 1) * MD],
                              start=(v == 0), stop=(v == NV - 1))
                mi.then_inc(mm.h, 1)

        @block.vector
        def _(v, sem=None):
            v.wait_ge(ld.h, 16)
            for t in range(NT):
                v.wait_ge(mm.h, t + 1)
                if t >= 2:
                    v.wait_ge(stq.h, 16 * (t - 1))
                v.tensor_tensor(hout[t % 2][:, :], ps[t % 2][:, :], brep[:, :],
                                ALU.add).then_inc(ev.h, 1)
    return nc



def _dft_consts_np():
    i = np.arange(128)
    f = np.arange(128)
    ang = 2 * np.pi * np.outer(i, f) / 256.0
    WfrT = np.cos(ang).astype(BF)
    WfiT = (-np.sin(ang)).astype(BF)
    alt = ((-1.0) ** i).astype(BF)[:, None]
    w = np.full(128, 2.0)
    w[0] = 1.0
    angl = 2 * np.pi * np.outer(f, np.arange(128)) / 256.0
    VcT = (w[:, None] * np.cos(angl) / 256.0).astype(BF)
    VsT = (-w[:, None] * np.sin(angl) / 256.0).astype(BF)
    altl = ((((-1.0) ** np.arange(128)) / 256.0).astype(BF))[None, :]
    ident = np.eye(128, dtype=np.float32)
    OH = np.zeros((128, 256), dtype=BF)
    for j in range(16):
        OH[:, j * 16 + j] = 1.0
    return WfrT, WfiT, alt, VcT, VsT, altl, ident, OH



class Chain:

    def __init__(self, nc, name="chain"):
        self.nc = nc
        self.csem = nc.alloc_semaphore(name + "_c")
        self.dsem = nc.alloc_semaphore(name + "_d")
        self.gsem = nc.alloc_semaphore(name + "_g")
        self.steps = []
        self.ct = 0
        self.dt = 0
        self.gt = 0

    def step(self, engine, inc=None):
        def deco(fn):
            self.steps.append((engine, fn))
            return fn
        return deco

    def emit(self, block=None):
        with self.nc.Block() as b:
            last = {}
            for (en, fn) in self.steps:
                def body(eng, fn=fn):
                    if self.ct > 0 and last.get("c") != (en, self.ct):
                        eng.wait_ge(self.csem, self.ct)
                    if self.dt > 0 and last.get("d") != (en, self.dt):
                        eng.wait_ge(self.dsem, self.dt)
                    if self.gt > 0 and last.get("g") != (en, self.gt):
                        eng.wait_ge(self.gsem, self.gt)
                    last["c"] = (en, self.ct)
                    last["d"] = (en, self.dt)
                    last["g"] = (en, self.gt)
                    fn(_Auto(self, eng), None)
                getattr(b, en)(body)

            def fin(g):
                g.wait_ge(self.csem, self.ct)
                g.wait_ge(self.dsem, self.dt)
                g.wait_ge(self.gsem, self.gt)
            b.gpsimd(fin)


_NOSYNC = {"wait_ge", "nop", "engine_nop", "register", "reg_mov", "print"}


class _Auto:
    def __init__(self, ch, eng):
        self._ch = ch
        self._eng = eng

    def __getattr__(self, name):
        target = getattr(self._eng, name)
        if name.startswith("_") or not callable(target) or name in _NOSYNC:
            return target

        def wrap(*a, **k):
            inst = target(*a, **k)
            if inst is None or not hasattr(inst, "then_inc"):
                return inst
            if name in ("dma_start", "dma_start_transpose", "collective_compute",
                        "indirect_dma_start"):
                from concourse.bass import BassGpSimd
                if isinstance(self._eng, BassGpSimd):
                    inst.then_inc(self._ch.gsem, 16)
                    self._ch.gt += 16
                else:
                    inst.then_inc(self._ch.dsem, 16)
                    self._ch.dt += 16
            else:
                inst.then_inc(self._ch.csem, 1)
                self._ch.ct += 1
                self._eng.wait_ge(self._ch.csem, self._ch.ct)
            return inst
        return wrap


def ap(tsr, off, *dims):
    return bass.AP(tsr, off, [list(d) for d in dims])


def build_mini_a():
    nc = bass.Bass("TRN2", target_bir_lowering=False)
    h = nc.dram_tensor("h", [T, MD], F32, kind="ExternalInput")
    lnw = nc.dram_tensor("lnw", [1, MD], F32, kind="ExternalInput")
    lnb = nc.dram_tensor("lnb", [1, MD], F32, kind="ExternalInput")
    scl = nc.dram_tensor("scl", [1, MD], F32, kind="ExternalInput")
    phzs = nc.dram_tensor("phzs", [128, 8], F32, kind="ExternalInput")
    phzrow = nc.dram_tensor("phzrow", [2, MD], F32, kind="ExternalInput")
    cinit = nc.dram_tensor("cinit", [128, 8], F32, kind="ExternalInput")
    hmid = nc.dram_tensor("hmid", [T, MD], F32, kind="ExternalOutput")
    cfin = nc.dram_tensor("cfin", [128, 8], F32, kind="ExternalOutput")
    Pp1o = nc.dram_tensor("Pp1o", [128, 2 * MD], BF16, kind="ExternalOutput")
    P128o = nc.dram_tensor("P128o", [128, 8], F32, kind="ExternalOutput")
    prow_scr = nc.dram_tensor("prow_scr", [2, MD], F32, kind="Internal")
    crow_scr = nc.dram_tensor("crow_scr", [2 * 16 * MD], F32, kind="Internal")

    WfrT_np, WfiT_np, alt_np, VcT_np, VsT_np, altl_np, ident_np, OH_np = _dft_consts_np()
    NCHK = 16

    with ExitStack() as st:
        sb = lambda n, sh, dt: st.enter_context(nc.sbuf_tensor(n, sh, dt))
        WfrT = sb("WfrT", [128, 128], BF16)
        WfiT = sb("WfiT", [128, 128], BF16)
        alt = sb("alt", [128, 1], BF16)
        VcT = sb("VcT", [128, 128], BF16)
        VsT = sb("VsT", [128, 128], BF16)
        altl = sb("altl", [1, 128], BF16)
        ident = sb("ident", [128, 128], F32)
        OH = sb("OH", [128, 256], BF16)
        lwrep = sb("lwrep", [128, MD], F32)
        lbrep = sb("lbrep", [128, MD], F32)
        sclrep = sb("sclrep", [128, MD], F32)
        prep_r = sb("prep_r", [128, MD], F32)
        prep_i = sb("prep_i", [128, MD], F32)
        pz = sb("pz", [128, 8], F32)
        psc = sb("psc", [128, 8], F32)
        przr = sb("przr", [1, MD], F32)
        przi = sb("przi", [1, MD], F32)
        rowsq = sb("rowsq", [1, MD], F32)
        rowtmp = sb("rowtmp", [1, MD], F32)
        prowpr = sb("prowpr", [1, MD], F32)
        prowpi = sb("prowpi", [1, MD], F32)
        tr8 = sb("tr8", [128, 8], F32)
        Pcm_r = sb("Pcm_r", [128, 4 * 132], F32)
        Pcm_i = sb("Pcm_i", [128, 4 * 132], F32)
        tmpA = sb("tmpA", [128, 4 * 64], F32)
        tmpB = sb("tmpB", [128, 4 * 64], F32)
        Pmc_r = sb("Pmc_r", [128, MD], F32)
        Pmc_i = sb("Pmc_i", [128, MD], F32)
        Pmc_rbf = sb("Pmc_rbf", [128, MD], BF16)
        Prev_r = sb("Prev_r", [128, MD], F32)
        Prev_i = sb("Prev_i", [128, MD], F32)
        frrev = sb("frrev", [128, MD], BF16)
        firev = sb("firev", [128, MD], BF16)
        Frr = sb("Frr", [128, MD], BF16)
        Fri = sb("Fri", [128, MD], BF16)
        Frr128 = sb("Frr128", [1, MD], BF16)
        Pp1r = sb("Pp1r", [128, MD], BF16)
        Pp1i = sb("Pp1i", [128, MD], BF16)
        tbig1 = sb("tbig1", [128, MD], F32)
        tbig2 = sb("tbig2", [128, MD], F32)
        P128 = sb("P128", [128, 8], F32)
        P128p = [sb(f"P128p{k}", [128, 8], F32) for k in range(4)]
        hsb = [sb(f"hsb{t}", [128, MD], F32) for t in range(NCHK)]
        conv_sb = [sb(f"conv{t}", [128, MD], F32) for t in range(NCHK)]
        ysb = sb("ysb", [128, MD], BF16)
        lnt = sb("lnt", [128, MD], F32)
        sums = sb("sums", [128, 1], F32)
        sumsq = sb("sumsq", [128, 1], F32)
        nmu = sb("nmu", [128, 1], F32)
        ve = sb("ve", [128, 1], F32)
        rstd = sb("rstd", [128, 1], F32)
        scr = sb("scr", [128, MD], F32)
        scr0 = sb("scr0", [128, 1], F32)
        Yr_sb = sb("Yr_sb", [128, MD], BF16)
        Yi_sb = sb("Yi_sb", [128, MD], BF16)
        yr128 = sb("yr128", [1, MD], BF16)
        Hr_sb = sb("Hr_sb", [128, MD], BF16)
        Hi_sb = sb("Hi_sb", [128, MD], BF16)
        Hr128 = sb("Hr128", [1, MD], BF16)
        tre = sb("tre", [128, MD], BF16)
        tim = sb("tim", [128, MD], BF16)
        tA = sb("tA", [128, MD], BF16)
        tB = sb("tB", [128, MD], BF16)
        U_r = sb("U_r", [128, 64], F32)
        U_i = sb("U_i", [128, 64], F32)
        Urow_r = sb("Urow_r", [16, MD], F32)
        Urow_i = sb("Urow_i", [16, MD], F32)
        PPc_r = sb("PPc_r", [128, 64], F32)
        PPc_i = sb("PPc_i", [128, 64], F32)
        ts1 = sb("ts1", [128, 64], F32)
        ts2 = sb("ts2", [128, 64], F32)
        ts3 = sb("ts3", [128, 64], F32)
        ts4 = sb("ts4", [128, 64], F32)
        csh_r = sb("csh_r", [128, 64], F32)
        csh_i = sb("csh_i", [128, 64], F32)
        crow_sb = sb("crow_sb", [16, 2 * MD], F32)
        ci_sb = sb("ci_sb", [128, 8], F32)
        cfin_sb = sb("cfin_sb", [128, 8], F32)
        crep = sb("crep", [128, 2 * 16 * MD], BF16)

        psm = lambda n, shp: st.enter_context(nc.psum_tensor(n, shp, F32))
        Yr_ps = psm("Yr_ps", [128, MD])
        Yi_ps = psm("Yi_ps", [128, MD])
        yrow_ps = psm("yrow_ps", [1, MD])
        conv_ps = psm("conv_ps", [128, MD])
        Ur_ps = psm("Ur_ps", [16, MD])
        Ui_ps = psm("Ui_ps", [16, MD])
        trU_ps = psm("trU_ps", [128, 128])
        trC_ps = psm("trC_ps", [128, 128])

        WfrT_d = nc.inline_tensor(np.asarray(WfrT_np), "WfrT_d")
        WfiT_d = nc.inline_tensor(np.asarray(WfiT_np), "WfiT_d")
        alt_d = nc.inline_tensor(np.asarray(alt_np), "alt_d")
        VcT_d = nc.inline_tensor(np.asarray(VcT_np), "VcT_d")
        VsT_d = nc.inline_tensor(np.asarray(VsT_np), "VsT_d")
        altl_d = nc.inline_tensor(np.asarray(altl_np), "altl_d")
        ident_d = nc.inline_tensor(ident_np, "ident_d")
        OH_d = nc.inline_tensor(np.asarray(OH_np), "OH_d")

        ch = Chain(nc)
        PIT = lambda t: t.shape[1]

        @ch.step("sync", 16 * 31)
        def _(sync, sem=None):
            n = 0
            for t in range(NCHK):
                sync.dma_start(hsb[t][:, :], h[128 * t:128 * (t + 1), :])
            for dst, src in [(lwrep, lnw), (lbrep, lnb), (sclrep, scl)]:
                sync.dma_start(dst[:, :], bass.AP(src, 0, [[0, 128], [1, MD]]))
            sync.dma_start(pz[:, :], phzs[:, :])
            sync.dma_start(przr[:, :], phzrow[0:1, :])
            sync.dma_start(przi[:, :], phzrow[1:2, :])
            sync.dma_start(ci_sb[:, :], cinit[:, :])
            for dst, src in [(WfrT, WfrT_d), (WfiT, WfiT_d), (alt, alt_d),
                             (VcT, VcT_d), (VsT, VsT_d), (altl, altl_d),
                             (ident, ident_d), (OH, OH_d)]:
                sync.dma_start(dst[:, :], src[:, :])
            return n

        @ch.step("vector")
        def _(v, sem=None):
            v.tensor_tensor(tr8[:, 0:4], pz[:, 0:4], pz[:, 0:4], ALU.mult)
            v.tensor_tensor(tr8[:, 4:8], pz[:, 4:8], pz[:, 4:8], ALU.mult)
            v.tensor_tensor(tr8[:, 0:4], tr8[:, 0:4], tr8[:, 4:8], ALU.mult if False else ALU.add)
            v.tensor_tensor(rowsq[:, :], przr[:, :], przr[:, :], ALU.mult)
            v.tensor_tensor(rowtmp[:, :], przi[:, :], przi[:, :], ALU.mult)
            return v.tensor_tensor(rowsq[:, :], rowsq[:, :], rowtmp[:, :], ALU.add)

        @ch.step("scalar")
        def _(s, sem=None):
            s.activation(tr8[:, 4:8], tr8[:, 0:4], AF.Sqrt)
            s.activation(tr8[:, 0:4], tr8[:, 4:8], AF.Tanh)
            s.activation(rowsq[:, :], rowsq[:, :], AF.Sqrt)
            return s.activation(rowtmp[:, :], rowsq[:, :], AF.Tanh)

        @ch.step("vector")
        def _(v, sem=None):
            v.reciprocal(tr8[:, 4:8], tr8[:, 4:8])
            v.tensor_tensor(tr8[:, 4:8], tr8[:, 4:8], tr8[:, 0:4], ALU.mult)
            v.tensor_tensor(psc[:, 0:4], pz[:, 0:4], tr8[:, 4:8], ALU.mult)
            v.tensor_tensor(psc[:, 4:8], pz[:, 4:8], tr8[:, 4:8], ALU.mult)
            v.reciprocal(rowsq[:, :], rowsq[:, :])
            v.tensor_tensor(rowsq[:, :], rowsq[:, :], rowtmp[:, :], ALU.mult)
            v.tensor_tensor(prowpr[:, :], przr[:, :], rowsq[:, :], ALU.mult)
            return v.tensor_tensor(prowpi[:, :], przi[:, :], rowsq[:, :], ALU.mult)

        @ch.step("gpsimd")
        def _(g, sem=None):
            g.dma_start(prow_scr[0:1, :], prowpr[:, :])
            g.dma_start(prow_scr[1:2, :], prowpi[:, :])

        @ch.step("gpsimd")
        def _(g, sem=None):
            g.dma_start(prep_r[:, :], bass.AP(prow_scr, 0, [[0, 128], [1, MD]]))
            g.dma_start(prep_i[:, :], bass.AP(prow_scr, MD, [[0, 128], [1, MD]]))

        @ch.step("vector")
        def _(v, sem=None):
            last = None
            for cb in range(4):
                o = cb * 132
                v.memset(Pcm_r[:, o:o + 1], 1.0)
                v.memset(Pcm_i[:, o:o + 1], 0.0)
                v.tensor_copy(Pcm_r[:, o + 1:o + 2], psc[:, cb:cb + 1])
                last = v.tensor_copy(Pcm_i[:, o + 1:o + 2], psc[:, 4 + cb:5 + cb])
            w = 1
            for k in range(7):
                sl = lambda tsr, off, cnt: ap(tsr, off, (PIT(tsr), 128), (132, 4), (1, cnt))
                Ar, Ai = sl(Pcm_r, 1, w), sl(Pcm_i, 1, w)
                Br = ap(Pcm_r, w, (PIT(Pcm_r), 128), (132, 4), (0, w))
                Bi = ap(Pcm_i, w, (PIT(Pcm_i), 128), (132, 4), (0, w))
                Or, Oi = sl(Pcm_r, w + 1, w), sl(Pcm_i, w + 1, w)
                t1 = ap(tmpA, 0, (PIT(tmpA), 128), (64, 4), (1, w))
                t2 = ap(tmpB, 0, (PIT(tmpB), 128), (64, 4), (1, w))
                v.tensor_tensor(t1, Ar, Br, ALU.mult)
                v.tensor_tensor(t2, Ai, Bi, ALU.mult)
                v.tensor_tensor(Or, t1, t2, ALU.subtract)
                v.tensor_tensor(t1, Ar, Bi, ALU.mult)
                v.tensor_tensor(t2, Ai, Br, ALU.mult)
                last = v.tensor_tensor(Oi, t1, t2, ALU.add)
                w *= 2
            return last

        for cb_ in range(4):
            for ri_ in range(2):
                def mk_t(cb=cb_, ri=ri_):
                    @ch.step("tensor")
                    def _(te, sem=None):
                        src = [Pcm_r, Pcm_i][ri]
                        return te.transpose(trU_ps[:, :], src[:, cb * 132:cb * 132 + 128],
                                            ident[:, :])

                    @ch.step("vector")
                    def _(v, sem=None):
                        dst = [Pmc_r, Pmc_i][ri]
                        return v.tensor_copy(dst[:, cb * 128:(cb + 1) * 128], trU_ps[:, :])
                mk_t()

        @ch.step("vector")
        def _(v, sem=None):
            v.tensor_copy(ap(Prev_r, 0, (PIT(Prev_r), 128), (128, 4), (1, 128)),
                          ap(Pcm_r, 127, (PIT(Pcm_r), 128), (132, 4), (-1, 128)))
            return v.tensor_copy(ap(Prev_i, 0, (PIT(Prev_i), 128), (128, 4), (1, 128)),
                                 ap(Pcm_i, 127, (PIT(Pcm_i), 128), (132, 4), (-1, 128)))

        for cb_ in range(4):
            for ri_ in range(2):
                def mk_t2(cb=cb_, ri=ri_):
                    @ch.step("tensor")
                    def _(te, sem=None):
                        src = [Prev_r, Prev_i][ri]
                        return te.transpose(trC_ps[:, :], src[:, cb * 128:(cb + 1) * 128],
                                            ident[:, :])

                    @ch.step("vector")
                    def _(v, sem=None):
                        dst = [frrev, firev][ri]
                        return v.tensor_copy(dst[:, cb * 128:(cb + 1) * 128], trC_ps[:, :])
                mk_t2()

        @ch.step("vector")
        def _(v, sem=None):
            return v.tensor_copy(Pmc_rbf[:, :], Pmc_r[:, :])

        @ch.step("tensor")
        def _(te, sem=None):
            te.matmul(Yr_ps[:, :], WfrT[:, :], Pmc_rbf[:, :], start=True, stop=True)
            te.matmul(Yi_ps[:, :], WfiT[:, :], Pmc_rbf[:, :], start=True, stop=True)
            return te.matmul(yrow_ps[0:1, :], alt[:, :], Pmc_rbf[:, :], start=True, stop=True)

        @ch.step("vector")
        def _(v, sem=None):
            v.tensor_tensor(Frr[:, :], Yr_ps[:, :], sclrep[:, :], ALU.mult)
            v.tensor_tensor(Fri[:, :], Yi_ps[:, :], sclrep[:, :], ALU.mult)
            return v.tensor_tensor(Frr128[0:1, :], yrow_ps[0:1, :], sclrep[0:1, :], ALU.mult)

        @ch.step("vector")
        def _(v, sem=None):
            v.tensor_tensor(tbig1[:, :], Pmc_r[:, :], prep_r[:, :], ALU.mult)
            v.tensor_tensor(tbig2[:, :], Pmc_i[:, :], prep_i[:, :], ALU.mult)
            v.tensor_tensor(tbig1[:, :], tbig1[:, :], tbig2[:, :], ALU.subtract)
            v.tensor_tensor(scr[:, :], Pmc_r[:, :], prep_i[:, :], ALU.mult)
            v.tensor_tensor(tbig2[:, :], Pmc_i[:, :], prep_r[:, :], ALU.mult)
            v.tensor_tensor(tbig2[:, :], scr[:, :], tbig2[:, :], ALU.add)
            v.tensor_tensor(Pp1r[:, :], tbig1[:, :], sclrep[:, :], ALU.mult)
            v.tensor_tensor(Pp1i[:, :], tbig2[:, :], sclrep[:, :], ALU.mult)
            v.tensor_copy(P128[:, 0:4], ap(Pcm_r, 128, (PIT(Pcm_r), 128), (132, 4)))
            v.tensor_copy(P128[:, 4:8], ap(Pcm_i, 128, (PIT(Pcm_i), 128), (132, 4)))
            last = v.tensor_copy(P128p[0][:, :], P128[:, :])
            for k in range(1, 4):
                a = P128p[k - 1]
                o = P128p[k]
                v.tensor_tensor(tr8[:, 0:4], a[:, 0:4], a[:, 0:4], ALU.mult)
                v.tensor_tensor(tr8[:, 4:8], a[:, 4:8], a[:, 4:8], ALU.mult)
                v.tensor_tensor(o[:, 0:4], tr8[:, 0:4], tr8[:, 4:8], ALU.subtract)
                v.tensor_tensor(tr8[:, 0:4], a[:, 0:4], a[:, 4:8], ALU.mult)
                last = v.tensor_tensor(o[:, 4:8], tr8[:, 0:4], tr8[:, 0:4], ALU.add)
            return last

        for j_ in range(NCHK):
            def mk_passa(j=j_):
                @ch.step("vector")
                def _(v, sem=None):
                    v.tensor_scalar(scr[:, :], hsb[j][:, :], 1.0, 0.0, ALU.mult, ALU.add,
                                    accum_out=sums[:, :])
                    v.scalar_tensor_tensor(scr[:, :], hsb[j][:, :], 1.0, hsb[j][:, :],
                                           ALU.mult, ALU.mult, accum_out=sumsq[:, :])
                    v.tensor_scalar(nmu[:, :], sums[:, :], -1.0 / MD, None, ALU.mult)
                    v.tensor_scalar(ve[:, :], sumsq[:, :], 1.0 / MD, EPS, ALU.mult, ALU.add)
                    v.tensor_tensor(scr0[:, :], nmu[:, :], nmu[:, :], ALU.mult)
                    v.tensor_tensor(ve[:, :], ve[:, :], scr0[:, :], ALU.subtract)
                    return v.reciprocal(ve[:, :], ve[:, :])

                @ch.step("scalar")
                def _(s, sem=None):
                    return s.activation(rstd[:, :], ve[:, :], AF.Sqrt)

                @ch.step("vector")
                def _(v, sem=None):
                    v.tensor_scalar(lnt[:, :], hsb[j][:, :], nmu[:, 0:1], rstd[:, 0:1],
                                    ALU.add, ALU.mult)
                    v.tensor_tensor(lnt[:, :], lnt[:, :], lwrep[:, :], ALU.mult)
                    return v.tensor_tensor(lnt[:, :], lnt[:, :], lbrep[:, :], ALU.add)

                @ch.step("scalar")
                def _(s, sem=None):
                    return s.activation(ysb[:, :], lnt[:, :], AF.Silu)

                @ch.step("tensor")
                def _(te, sem=None):
                    te.matmul(Yr_ps[:, :], WfrT[:, :], ysb[:, :], start=True, stop=True)
                    te.matmul(Yi_ps[:, :], WfiT[:, :], ysb[:, :], start=True, stop=True)
                    return te.matmul(yrow_ps[0:1, :], alt[:, :], ysb[:, :],
                                     start=True, stop=True)

                @ch.step("scalar")
                def _(s, sem=None):
                    s.copy(Yr_sb[:, :], Yr_ps[:, :])
                    s.copy(Yi_sb[:, :], Yi_ps[:, :])
                    return s.copy(yr128[0:1, :], yrow_ps[0:1, :])

                @ch.step("vector")
                def _(v, sem=None):
                    v.tensor_tensor(tA[:, :], Yr_sb[:, :], Frr[:, :], ALU.mult)
                    v.tensor_tensor(tB[:, :], Yi_sb[:, :], Fri[:, :], ALU.mult)
                    v.tensor_tensor(Hr_sb[:, :], tA[:, :], tB[:, :], ALU.subtract)
                    v.tensor_tensor(tA[:, :], Yr_sb[:, :], Fri[:, :], ALU.mult)
                    v.tensor_tensor(tB[:, :], Yi_sb[:, :], Frr[:, :], ALU.mult)
                    v.tensor_tensor(Hi_sb[:, :], tA[:, :], tB[:, :], ALU.add)
                    v.tensor_tensor(Hr128[0:1, :], yr128[0:1, :], Frr128[0:1, :], ALU.mult)
                    v.tensor_tensor(tre[:, :], ysb[:, :], frrev[:, :], ALU.mult)
                    return v.tensor_tensor(tim[:, :], ysb[:, :], firev[:, :], ALU.mult)

                @ch.step("tensor")
                def _(te, sem=None):
                    te.matmul(Ur_ps[0:16, :], OH[:, j * 16:(j + 1) * 16], tre[:, :],
                              start=(j == 0), stop=(j == NCHK - 1), skip_group_check=True)
                    te.matmul(Ui_ps[0:16, :], OH[:, j * 16:(j + 1) * 16], tim[:, :],
                              start=(j == 0), stop=(j == NCHK - 1), skip_group_check=True)
                    te.matmul(conv_ps[:, :], VcT[:, :], Hr_sb[:, :], start=True, stop=False)
                    te.matmul(conv_ps[:, :], VsT[:, :], Hi_sb[:, :], start=False, stop=False)
                    return te.matmul(conv_ps[:, :], altl[0:1, :], Hr128[0:1, :],
                                     start=False, stop=True)

                @ch.step("scalar")
                def _(s, sem=None):
                    return s.copy(conv_sb[j][:, :], conv_ps[:, :])
            mk_passa()

        @ch.step("vector")
        def _(v, sem=None):
            v.tensor_copy(Urow_r[:, :], Ur_ps[:, :])
            return v.tensor_copy(Urow_i[:, :], Ui_ps[:, :])

        for cb_ in range(4):
            for ri_ in range(2):
                def mk_t3(cb=cb_, ri=ri_):
                    @ch.step("tensor")
                    def _(te, sem=None):
                        src = [Urow_r, Urow_i][ri]
                        return te.transpose(trU_ps[0:128, 0:16],
                                            src[0:16, cb * 128:(cb + 1) * 128],
                                            ident[0:16, 0:16])

                    @ch.step("vector")
                    def _(v, sem=None):
                        dst = [U_r, U_i][ri]
                        return v.tensor_copy(dst[:, cb * 16:(cb + 1) * 16],
                                             trU_ps[0:128, 0:16])
                mk_t3()

        @ch.step("vector")
        def _(v, sem=None):
            w = 1
            for k in range(4):
                n = 16 - w
                sl = lambda tsr, off, cnt=None: ap(tsr, off, (PIT(tsr), 128), (16, 4), (1, n))
                zsr, zsi = sl(U_r, 0), sl(U_i, 0)
                tgr, tgi = sl(U_r, w), sl(U_i, w)
                Pr = ap(P128p[k], 0, (8, 128), (1, 4), (0, n))
                Pi = ap(P128p[k], 4, (8, 128), (1, 4), (0, n))
                t1 = ap(ts1, 0, (PIT(ts1), 128), (16, 4), (1, n))
                t2 = ap(ts2, 0, (PIT(ts2), 128), (16, 4), (1, n))
                t3 = ap(ts3, 0, (PIT(ts3), 128), (16, 4), (1, n))
                t4 = ap(ts4, 0, (PIT(ts4), 128), (16, 4), (1, n))
                v.tensor_tensor(t1, zsr, Pr, ALU.mult)
                v.tensor_tensor(t2, zsi, Pi, ALU.mult)
                v.tensor_tensor(t3, zsr, Pi, ALU.mult)
                v.tensor_tensor(t4, zsi, Pr, ALU.mult)
                v.tensor_tensor(t1, t1, t2, ALU.subtract)
                v.tensor_tensor(t3, t3, t4, ALU.add)
                v.tensor_tensor(tgr, tgr, t1, ALU.add)
                v.tensor_tensor(tgi, tgi, t3, ALU.add)
                w *= 2
            v.tensor_copy(ap(PPc_r, 0, (PIT(PPc_r), 128), (16, 4)), P128[:, 0:4])
            v.tensor_copy(ap(PPc_i, 0, (PIT(PPc_i), 128), (16, 4)), P128[:, 4:8])
            w = 1
            for k in range(4):
                n = min(2 * w, 16) - w
                sl2 = lambda tsr, off: ap(tsr, off, (PIT(tsr), 128), (16, 4), (1, n))
                ar, ai = sl2(PPc_r, 0), sl2(PPc_i, 0)
                tgr, tgi = sl2(PPc_r, w), sl2(PPc_i, w)
                Pr = ap(P128p[k], 0, (8, 128), (1, 4), (0, n))
                Pi = ap(P128p[k], 4, (8, 128), (1, 4), (0, n))
                t1 = ap(ts1, 0, (PIT(ts1), 128), (16, 4), (1, n))
                t2 = ap(ts2, 0, (PIT(ts2), 128), (16, 4), (1, n))
                v.tensor_tensor(t1, ar, Pr, ALU.mult)
                v.tensor_tensor(t2, ai, Pi, ALU.mult)
                v.tensor_tensor(tgr, t1, t2, ALU.subtract)
                v.tensor_tensor(t1, ar, Pi, ALU.mult)
                v.tensor_tensor(t2, ai, Pr, ALU.mult)
                v.tensor_tensor(tgi, t1, t2, ALU.add)
                w *= 2
            cir = ap(ci_sb, 0, (8, 128), (1, 4), (0, 16))
            cii = ap(ci_sb, 4, (8, 128), (1, 4), (0, 16))
            sl3 = lambda tsr: ap(tsr, 0, (PIT(tsr), 128), (16, 4), (1, 16))
            v.tensor_tensor(sl3(ts1), sl3(PPc_r), cir, ALU.mult)
            v.tensor_tensor(sl3(ts2), sl3(PPc_i), cii, ALU.mult)
            v.tensor_tensor(sl3(ts1), sl3(ts1), sl3(ts2), ALU.subtract)
            v.tensor_tensor(sl3(U_r), sl3(U_r), sl3(ts1), ALU.add)
            v.tensor_tensor(sl3(ts1), sl3(PPc_r), cii, ALU.mult)
            v.tensor_tensor(sl3(ts2), sl3(PPc_i), cir, ALU.mult)
            v.tensor_tensor(sl3(ts1), sl3(ts1), sl3(ts2), ALU.add)
            v.tensor_tensor(sl3(U_i), sl3(U_i), sl3(ts1), ALU.add)
            v.tensor_copy(cfin_sb[:, 0:4], ap(U_r, 15, (PIT(U_r), 128), (16, 4)))
            v.tensor_copy(cfin_sb[:, 4:8], ap(U_i, 15, (PIT(U_i), 128), (16, 4)))
            v.tensor_copy(ap(csh_r, 0, (PIT(csh_r), 128), (16, 4)), ci_sb[:, 0:4])
            v.tensor_copy(ap(csh_i, 0, (PIT(csh_i), 128), (16, 4)), ci_sb[:, 4:8])
            v.tensor_copy(ap(csh_r, 1, (PIT(csh_r), 128), (16, 4), (1, 15)),
                          ap(U_r, 0, (PIT(U_r), 128), (16, 4), (1, 15)))
            return v.tensor_copy(ap(csh_i, 1, (PIT(csh_i), 128), (16, 4), (1, 15)),
                                 ap(U_i, 0, (PIT(U_i), 128), (16, 4), (1, 15)))

        for cb_ in range(4):
            for ri_ in range(2):
                def mk_t4(cb=cb_, ri=ri_):
                    @ch.step("tensor")
                    def _(te, sem=None):
                        src = [csh_r, csh_i][ri]
                        return te.transpose(trC_ps[0:16, 0:128],
                                            src[:, cb * 16:(cb + 1) * 16], ident[:, :])

                    @ch.step("vector")
                    def _(v, sem=None):
                        return v.tensor_copy(
                            crow_sb[0:16, ri * MD + cb * 128: ri * MD + (cb + 1) * 128],
                            trC_ps[0:16, 0:128])
                mk_t4()

        @ch.step("sync")
        def _(sync, sem=None):
            return sync.dma_start(
                ap(crow_scr, 0, (512, 16), (8192, 2), (1, 512)),
                ap(crow_sb, 0, (PIT(crow_sb), 16), (MD, 2), (1, MD)))

        @ch.step("gpsimd")
        def _(g, sem=None):
            g.dma_start(crep[:, :], bass.AP(crow_scr, 0, [[0, 128], [1, 16384]]))
            g.dma_start(cfin[:, :], cfin_sb[:, :])
            g.dma_start(Pp1o[:, 0:MD], Pp1r[:, :])
            g.dma_start(Pp1o[:, MD:2 * MD], Pp1i[:, :])
            g.dma_start(P128o[:, :], P128[:, :])

        for j_ in range(NCHK):
            def mk_passb(j=j_):
                @ch.step("vector")
                def _(v, sem=None):
                    v.tensor_tensor(tbig1[:, :], crep[:, j * MD:(j + 1) * MD],
                                    Pp1r[:, :], ALU.mult)
                    v.tensor_tensor(tbig2[:, :], crep[:, 8192 + j * MD: 8192 + (j + 1) * MD],
                                    Pp1i[:, :], ALU.mult)
                    v.tensor_tensor(tbig1[:, :], tbig1[:, :], tbig2[:, :], ALU.subtract)
                    v.tensor_tensor(hsb[j][:, :], hsb[j][:, :], conv_sb[j][:, :], ALU.add)
                    return v.tensor_tensor(hsb[j][:, :], hsb[j][:, :], tbig1[:, :], ALU.add)

                @ch.step("sync")
                def _(sync, sem=None):
                    return sync.dma_start(hmid[128 * j:128 * (j + 1), :], hsb[j][:, :])
            mk_passb()

        ch.emit()
    return nc

def build_mini_b():
    nc = bass.Bass("TRN2", target_bir_lowering=False)
    hmid = nc.dram_tensor("hmid", [T, MD], F32, kind="ExternalInput")
    cextra = nc.dram_tensor("cextra", [128, 8], F32, kind="ExternalInput")
    Pp1s = nc.dram_tensor("Pp1s", [128, 2 * MD], BF16, kind="ExternalInput")
    P128i = nc.dram_tensor("P128i", [128, 8], F32, kind="ExternalInput")
    ln2w = nc.dram_tensor("ln2w", [1, MD], F32, kind="ExternalInput")
    ln2b = nc.dram_tensor("ln2b", [1, MD], F32, kind="ExternalInput")
    W1T = nc.dram_tensor("W1T", [MD, MF], F32, kind="ExternalInput")
    b1c = nc.dram_tensor("b1c", [128, MF // 128], F32, kind="ExternalInput")
    W2T = nc.dram_tensor("W2T", [MF, MD], F32, kind="ExternalInput")
    b2 = nc.dram_tensor("b2", [1, MD], F32, kind="ExternalInput")
    hpost = nc.dram_tensor("hpost", [T, MD], F32, kind="ExternalOutput")
    mrow_scr = nc.dram_tensor("mrow_scr", [2 * 16 * MD], F32, kind="Internal")
    lny_scr = nc.dram_tensor("lny_scr", [T, MD], BF16, kind="Internal")
    NCHK = 16
    NF = MF // 128

    ident_np = np.eye(128, dtype=np.float32)

    with ExitStack() as st:
        sb = lambda n, sh, dt: st.enter_context(nc.sbuf_tensor(n, sh, dt))
        ident = sb("ident", [128, 128], F32)
        lwrep = sb("lwrep", [128, MD], F32)
        lbrep = sb("lbrep", [128, MD], F32)
        b2rep = sb("b2rep", [128, MD], F32)
        b1sb = sb("b1sb", [128, NF], F32)
        ce = sb("ce", [128, 8], F32)
        P128 = sb("P128", [128, 8], F32)
        P128p = [sb(f"P128p{k}", [128, 8], F32) for k in range(4)]
        tr8 = sb("tr8", [128, 8], F32)
        MMr = sb("MMr", [128, 64], F32)
        MMi = sb("MMi", [128, 64], F32)
        ts1 = sb("ts1", [128, 64], F32)
        ts2 = sb("ts2", [128, 64], F32)
        mrow = sb("mrow", [16, 2 * MD], F32)
        mrep = sb("mrep", [128, 2 * 16 * MD], BF16)
        Pp1r = sb("Pp1r", [128, MD], BF16)
        Pp1i = sb("Pp1i", [128, MD], BF16)
        hsb = [sb(f"hsb{t}", [128, MD], F32) for t in range(NCHK)]
        tbig1 = sb("tbig1", [128, MD], F32)
        tbig2 = sb("tbig2", [128, MD], F32)
        scr = sb("scr", [128, MD], F32)
        scr0 = sb("scr0", [128, 1], F32)
        sums = sb("sums", [128, 1], F32)
        sumsq = sb("sumsq", [128, 1], F32)
        nmu = sb("nmu", [128, 1], F32)
        ve = sb("ve", [128, 1], F32)
        rstd = sb("rstd", [128, 1], F32)
        lnt = sb("lnt", [128, MD], F32)
        lny = sb("lny", [128, MD], BF16)
        lnyT = [sb(f"lnyT{cb}", [128, T], BF16) for cb in range(4)]
        w1sb = [sb(f"w1sb{cb}", [128, MF], BF16) for cb in range(4)]
        y2sb = [sb(f"y2sb{f}", [128, T], BF16) for f in range(NF)]
        w2sb = [sb(f"w2sb{f}", [128, MD], BF16) for f in range(NF)]
        psum_y2 = st.enter_context(nc.psum_tensor("psum_y2", [128, T], F32))
        psum_h = st.enter_context(nc.psum_tensor("psum_h", [128, MD], F32))
        trC_ps = st.enter_context(nc.psum_tensor("trC_ps", [128, 128], F32))
        ident_d = nc.inline_tensor(ident_np, "ident_d")

        ch = Chain(nc)
        PIT = lambda t: t.shape[1]

        @ch.step("sync")
        def _(sync, sem=None):
            for t in range(NCHK):
                sync.dma_start(hsb[t][:, :], hmid[128 * t:128 * (t + 1), :])
            for dst, src in [(lwrep, ln2w), (lbrep, ln2b), (b2rep, b2)]:
                sync.dma_start(dst[:, :], bass.AP(src, 0, [[0, 128], [1, MD]]))
            sync.dma_start(ce[:, :], cextra[:, :])
            sync.dma_start(P128[:, :], P128i[:, :])
            sync.dma_start(b1sb[:, :], b1c[:, :])
            sync.dma_start(Pp1r[:, :], Pp1s[:, 0:MD])
            sync.dma_start(Pp1i[:, :], Pp1s[:, MD:2 * MD])
            sync.dma_start(ident[:, :], ident_d[:, :])

        @ch.step("gpsimd")
        def _(g, sem=None):
            for cb in range(4):
                g.dma_start(w1sb[cb][:, :], W1T[128 * cb:128 * (cb + 1), :])
            for f in range(NF):
                g.dma_start(w2sb[f][:, :], W2T[128 * f:128 * (f + 1), :])

        @ch.step("vector")
        def _(v, sem=None):
            v.tensor_copy(P128p[0][:, :], P128[:, :])
            for k in range(1, 4):
                a = P128p[k - 1]
                o = P128p[k]
                v.tensor_tensor(tr8[:, 0:4], a[:, 0:4], a[:, 0:4], ALU.mult)
                v.tensor_tensor(tr8[:, 4:8], a[:, 4:8], a[:, 4:8], ALU.mult)
                v.tensor_tensor(o[:, 0:4], tr8[:, 0:4], tr8[:, 4:8], ALU.subtract)
                v.tensor_tensor(tr8[:, 0:4], a[:, 0:4], a[:, 4:8], ALU.mult)
                v.tensor_tensor(o[:, 4:8], tr8[:, 0:4], tr8[:, 0:4], ALU.add)
            v.tensor_copy(ap(MMr, 0, (PIT(MMr), 128), (16, 4)), ce[:, 0:4])
            v.tensor_copy(ap(MMi, 0, (PIT(MMi), 128), (16, 4)), ce[:, 4:8])
            w = 1
            for k in range(4):
                n = min(2 * w, 16) - w
                sl2 = lambda tsr, off: ap(tsr, off, (PIT(tsr), 128), (16, 4), (1, n))
                ar, ai = sl2(MMr, 0), sl2(MMi, 0)
                tgr, tgi = sl2(MMr, w), sl2(MMi, w)
                Pr = ap(P128p[k], 0, (8, 128), (1, 4), (0, n))
                Pi = ap(P128p[k], 4, (8, 128), (1, 4), (0, n))
                t1 = ap(ts1, 0, (PIT(ts1), 128), (16, 4), (1, n))
                t2 = ap(ts2, 0, (PIT(ts2), 128), (16, 4), (1, n))
                v.tensor_tensor(t1, ar, Pr, ALU.mult)
                v.tensor_tensor(t2, ai, Pi, ALU.mult)
                v.tensor_tensor(tgr, t1, t2, ALU.subtract)
                v.tensor_tensor(t1, ar, Pi, ALU.mult)
                v.tensor_tensor(t2, ai, Pr, ALU.mult)
                v.tensor_tensor(tgi, t1, t2, ALU.add)
                w *= 2

        for cb_ in range(4):
            for ri_ in range(2):
                def mk_t(cb=cb_, ri=ri_):
                    @ch.step("tensor")
                    def _(te, sem=None):
                        src = [MMr, MMi][ri]
                        te.transpose(trC_ps[0:16, 0:128],
                                     src[:, cb * 16:(cb + 1) * 16], ident[:, :])

                    @ch.step("vector")
                    def _(v, sem=None):
                        v.tensor_copy(mrow[0:16, ri * MD + cb * 128: ri * MD + (cb + 1) * 128],
                                      trC_ps[0:16, 0:128])
                mk_t()

        @ch.step("sync")
        def _(sync, sem=None):
            sync.dma_start(ap(mrow_scr, 0, (512, 16), (8192, 2), (1, 512)),
                           ap(mrow, 0, (PIT(mrow), 16), (MD, 2), (1, MD)))

        @ch.step("gpsimd")
        def _(g, sem=None):
            g.dma_start(mrep[:, :], bass.AP(mrow_scr, 0, [[0, 128], [1, 16384]]))

        for j_ in range(NCHK):
            def mk_c(j=j_):
                @ch.step("vector")
                def _(v, sem=None):
                    v.tensor_tensor(tbig1[:, :], mrep[:, j * MD:(j + 1) * MD],
                                    Pp1r[:, :], ALU.mult)
                    v.tensor_tensor(tbig2[:, :], mrep[:, 8192 + j * MD:8192 + (j + 1) * MD],
                                    Pp1i[:, :], ALU.mult)
                    v.tensor_tensor(tbig1[:, :], tbig1[:, :], tbig2[:, :], ALU.subtract)
                    v.tensor_tensor(hsb[j][:, :], hsb[j][:, :], tbig1[:, :], ALU.add)
                    v.tensor_scalar(scr[:, :], hsb[j][:, :], 1.0, 0.0, ALU.mult, ALU.add,
                                    accum_out=sums[:, :])
                    v.scalar_tensor_tensor(scr[:, :], hsb[j][:, :], 1.0, hsb[j][:, :],
                                           ALU.mult, ALU.mult, accum_out=sumsq[:, :])
                    v.tensor_scalar(nmu[:, :], sums[:, :], -1.0 / MD, None, ALU.mult)
                    v.tensor_scalar(ve[:, :], sumsq[:, :], 1.0 / MD, EPS, ALU.mult, ALU.add)
                    v.tensor_tensor(scr0[:, :], nmu[:, :], nmu[:, :], ALU.mult)
                    v.tensor_tensor(ve[:, :], ve[:, :], scr0[:, :], ALU.subtract)
                    v.reciprocal(ve[:, :], ve[:, :])

                @ch.step("scalar")
                def _(s, sem=None):
                    s.activation(rstd[:, :], ve[:, :], AF.Sqrt)

                @ch.step("vector")
                def _(v, sem=None):
                    v.tensor_scalar(lnt[:, :], hsb[j][:, :], nmu[:, 0:1], rstd[:, 0:1],
                                    ALU.add, ALU.mult)
                    v.tensor_tensor(lnt[:, :], lnt[:, :], lwrep[:, :], ALU.mult)
                    v.tensor_tensor(lny[:, :], lnt[:, :], lbrep[:, :], ALU.add)

                @ch.step("sync")
                def _(sync, sem=None):
                    sync.dma_start(lny_scr[128 * j:128 * (j + 1), :], lny[:, :])
            mk_c()

        @ch.step("sync")
        def _(sync, sem=None):
            for cb in range(4):
                sync.dma_start(lnyT[cb][:, :], lny_scr[:, 128 * cb:128 * (cb + 1)],
                               transpose=True)

        for f_ in range(NF):
            def mk_w1(f=f_):
                @ch.step("tensor")
                def _(te, sem=None):
                    for tq in range(4):
                        for cb in range(4):
                            te.matmul(psum_y2[:, 512 * tq:512 * (tq + 1)],
                                      w1sb[cb][:, 128 * f:128 * (f + 1)],
                                      lnyT[cb][:, 512 * tq:512 * (tq + 1)],
                                      start=(cb == 0), stop=(cb == 3))

                @ch.step("scalar")
                def _(s, sem=None):
                    s.activation(y2sb[f][:, :], psum_y2[:, :], AF.Silu,
                                 bias=b1sb[:, f:f + 1])
            mk_w1()

        for t_ in range(NCHK):
            def mk_w2(t=t_):
                @ch.step("tensor")
                def _(te, sem=None):
                    for f in range(NF):
                        te.matmul(psum_h[:, :],
                                  y2sb[f][:, 128 * t:128 * (t + 1)],
                                  w2sb[f][:, :],
                                  start=(f == 0), stop=(f == NF - 1))

                @ch.step("vector")
                def _(v, sem=None):
                    v.tensor_tensor(scr[:, :], psum_h[:, :], b2rep[:, :], ALU.add)
                    v.tensor_tensor(hsb[t][:, :], hsb[t][:, :], scr[:, :], ALU.add)

                @ch.step("sync")
                def _(sync, sem=None):
                    sync.dma_start(hpost[128 * t:128 * (t + 1), :], hsb[t][:, :])
            mk_w2()

        ch.emit()
    return nc

def _emit_ln_chunk(ch, hsb_j, lwrep, lbrep, out_bf, tmp):
    scr, sums, sumsq, nmu, ve, rstd, scr0, lnt = tmp

    @ch.step("vector")
    def _(v, sem=None):
        v.tensor_scalar(scr[:, :], hsb_j[:, :], 1.0, 0.0, ALU.mult, ALU.add,
                        accum_out=sums[:, :])
        v.scalar_tensor_tensor(scr[:, :], hsb_j[:, :], 1.0, hsb_j[:, :],
                               ALU.mult, ALU.mult, accum_out=sumsq[:, :])
        v.tensor_scalar(nmu[:, :], sums[:, :], -1.0 / hsb_j.shape[1], None, ALU.mult)
        v.tensor_scalar(ve[:, :], sumsq[:, :], 1.0 / hsb_j.shape[1], EPS,
                        ALU.mult, ALU.add)
        v.tensor_tensor(scr0[:, :], nmu[:, :], nmu[:, :], ALU.mult)
        v.tensor_tensor(ve[:, :], ve[:, :], scr0[:, :], ALU.subtract)
        v.reciprocal(ve[:, :], ve[:, :])

    @ch.step("scalar")
    def _(s, sem=None):
        s.activation(rstd[:, :], ve[:, :], AF.Sqrt)

    @ch.step("vector")
    def _(v, sem=None):
        v.tensor_scalar(lnt[:, :], hsb_j[:, :], nmu[:, 0:1], rstd[:, 0:1],
                        ALU.add, ALU.mult)
        v.tensor_tensor(lnt[:, :], lnt[:, :], lwrep[:, :], ALU.mult)
        v.tensor_tensor(out_bf[:, :], lnt[:, :], lbrep[:, :], ALU.add)


def build_sfinal():
    nc = bass.Bass("TRN2", target_bir_lowering=False)
    h = nc.dram_tensor("h", [T, MD], F32, kind="ExternalInput")
    lnw = nc.dram_tensor("lnw", [1, MD], F32, kind="ExternalInput")
    lnb = nc.dram_tensor("lnb", [1, MD], F32, kind="ExternalInput")
    WoutT = nc.dram_tensor("WoutT", [MD, D], F32, kind="ExternalInput")
    bout = nc.dram_tensor("bout", [1, D], F32, kind="ExternalInput")
    s0 = nc.dram_tensor("s0", [T, D], BF16, kind="ExternalOutput")
    lny_scr = nc.dram_tensor("lny_scr", [T, MD], BF16, kind="Internal")
    NCHK = 16
    with ExitStack() as st:
        sb = lambda n, sh, dt: st.enter_context(nc.sbuf_tensor(n, sh, dt))
        lwrep = sb("lwrep", [128, MD], F32)
        lbrep = sb("lbrep", [128, MD], F32)
        borep = sb("borep", [128, D], F32)
        hsb = sb("hsb", [128, MD], F32)
        lny = sb("lny", [128, MD], BF16)
        hnT = [sb(f"hnT{cb}", [128, T], BF16) for cb in range(4)]
        wsb = [sb(f"wsb{cb}", [128, D], BF16) for cb in range(4)]
        sout = sb("sout", [128, D], BF16)
        tmp = [sb("scr", [128, MD], F32), sb("sums", [128, 1], F32),
               sb("sumsq", [128, 1], F32), sb("nmu", [128, 1], F32),
               sb("ve", [128, 1], F32), sb("rstd", [128, 1], F32),
               sb("scr0", [128, 1], F32), sb("lnt", [128, MD], F32)]
        psum = st.enter_context(nc.psum_tensor("psum", [128, D], F32))
        ch = Chain(nc)

        @ch.step("sync")
        def _(sync, sem=None):
            sync.dma_start(lwrep[:, :], bass.AP(lnw, 0, [[0, 128], [1, MD]]))
            sync.dma_start(lbrep[:, :], bass.AP(lnb, 0, [[0, 128], [1, MD]]))
            sync.dma_start(borep[:, :], bass.AP(bout, 0, [[0, 128], [1, D]]))

        @ch.step("gpsimd")
        def _(g, sem=None):
            for cb in range(4):
                g.dma_start(wsb[cb][:, :], WoutT[128 * cb:128 * (cb + 1), :])

        for j_ in range(NCHK):
            def mk(j=j_):
                @ch.step("sync")
                def _(sync, sem=None):
                    sync.dma_start(hsb[:, :], h[128 * j:128 * (j + 1), :])
                _emit_ln_chunk(ch, hsb, lwrep, lbrep, lny, tmp)

                @ch.step("sync")
                def _(sync, sem=None):
                    sync.dma_start(lny_scr[128 * j:128 * (j + 1), :], lny[:, :])
            mk()

        @ch.step("sync")
        def _(sync, sem=None):
            for cb in range(4):
                sync.dma_start(hnT[cb][:, :], lny_scr[:, 128 * cb:128 * (cb + 1)],
                               transpose=True)

        for t_ in range(NCHK):
            def mk2(t=t_):
                @ch.step("tensor")
                def _(te, sem=None):
                    for oq in range(8):
                        for cb in range(4):
                            te.matmul(psum[:, 512 * oq:512 * (oq + 1)],
                                      hnT[cb][:, 128 * t:128 * (t + 1)],
                                      wsb[cb][:, 512 * oq:512 * (oq + 1)],
                                      start=(cb == 0), stop=(cb == 3))

                @ch.step("vector")
                def _(v, sem=None):
                    v.tensor_tensor(sout[:, :], psum[:, :], borep[:, :], ALU.add)

                @ch.step("sync")
                def _(sync, sem=None):
                    sync.dma_start(s0[128 * t:128 * (t + 1), :], sout[:, :])
            mk2()
        ch.emit()
    return nc


def build_gate0():
    nc = bass.Bass("TRN2", target_bir_lowering=False)
    xnT = nc.dram_tensor("xnT", [D, T], BF16, kind="ExternalInput")
    qw = nc.dram_tensor("qw", [512, D], I32, kind="ExternalInput")
    qz = nc.dram_tensor("qz", [32, 512], I32, kind="ExternalInput")
    scales = nc.dram_tensor("scales", [32, D], F16, kind="ExternalInput")
    biasc = nc.dram_tensor("biasc", [128, 32], F16, kind="ExternalInput")
    shamt = nc.dram_tensor("shamt", [128, 1], I32, kind="ExternalInput")
    g0T = nc.dram_tensor("g0T", [D, T], BF16, kind="ExternalOutput")
    w4_scr = nc.dram_tensor("w4_scr", [D, D], BF16, kind="Internal")
    NG = 32
    with ExitStack() as st:
        sb = lambda n, sh, dt: st.enter_context(nc.sbuf_tensor(n, sh, dt))
        xsb = [sb(f"xsb{g}", [128, T], BF16) for g in range(NG)]
        qwrep = sb("qwrep", [128, D], I32)
        srep = sb("srep", [128, D], BF16)
        w4g = sb("w4g", [128, D], BF16)
        sh_sb = sb("sh_sb", [128, 1], I32)
        ztmp = sb("ztmp", [32, 512], I32)
        ssb = sb("ssb", [32, D], BF16)
        zsn = sb("zsn", [32, D], BF16)
        qzsb = sb("qzsb", [32, 512], I32)
        OHsb = sb("OHsb", [128, 32 * 32], BF16)
        xs_sb = sb("xs_sb", [32, T], BF16)
        bsb = sb("bsb", [128, 32], F16)
        bsf = sb("bsf", [128, 32], F32)
        w4c = sb("w4c", [128, D], BF16)
        xs_ps = st.enter_context(nc.psum_tensor("xs_ps", [32, T], F32))
        g_ps = st.enter_context(nc.psum_tensor("g_ps", [128, T], F32))
        OH32_np = np.zeros((128, 32 * 32), dtype=BF)
        for g in range(32):
            OH32_np[:, g * 32 + g] = 1.0
        OH32_d = nc.inline_tensor(np.asarray(OH32_np), "OH32_d")
        ch = Chain(nc)

        @ch.step("sync")
        def _(sync, sem=None):
            for g in range(NG):
                sync.dma_start(xsb[g][:, :], xnT[128 * g:128 * (g + 1), :])
            sync.dma_start(qzsb[:, :], qz[:, :])
            sync.dma_start(sh_sb[:, :], shamt[:, :])
            sync.dma_start(OHsb[:, :], OH32_d[:, :])
            sync.dma_start(bsb[:, :], biasc[:, :])

        @ch.step("gpsimd")
        def _(gp, sem=None):
            gp.dma_start(ssb[0:32, :], scales[:, :])

        @ch.step("vector")
        def _(v, sem=None):
            for nib in range(8):
                v.tensor_scalar(ztmp[0:32, :], qzsb[0:32, :], 4 * nib, 15,
                                ALU.logical_shift_right, ALU.bitwise_and)
                v.tensor_copy(ap(zsn, nib, (zsn.shape[1], 32), (8, 512)),
                              ztmp[0:32, :])
            v.tensor_tensor(zsn[0:32, :], zsn[0:32, :], ssb[0:32, :], ALU.mult)
            v.tensor_scalar(zsn[0:32, :], zsn[0:32, :], -1.0, None, ALU.mult)
            v.tensor_copy(bsf[:, :], bsb[:, :])

        @ch.step("tensor")
        def _(te, sem=None):
            for g in range(NG):
                for tq in range(4):
                    te.matmul(xs_ps[0:32, 512 * tq:512 * (tq + 1)],
                              OHsb[:, 32 * g:32 * (g + 1)],
                              xsb[g][:, 512 * tq:512 * (tq + 1)],
                              start=(g == 0), stop=(g == NG - 1),
                              skip_group_check=True)

        @ch.step("vector")
        def _(v, sem=None):
            v.tensor_copy(xs_sb[0:32, :], xs_ps[0:32, :])

        for g_ in range(NG):
            def mkg(g=g_):
                @ch.step("sync")
                def _(sync, sem=None):
                    sync.dma_start(qwrep[:, :],
                                   bass.AP(qw, 16 * g * D, [[D, 16], [0, 8], [1, D]]))

                @ch.step("gpsimd")
                def _(gp, sem=None):
                    gp.dma_start(srep[:, :], bass.AP(scales, g * D, [[0, 128], [1, D]]))

                @ch.step("vector")
                def _(v, sem=None):
                    v.tensor_scalar(qwrep[:, :], qwrep[:, :], sh_sb[:, 0:1], 15,
                                    ALU.logical_shift_right, ALU.bitwise_and)
                    v.tensor_copy(w4g[:, :], qwrep[:, :])
                    v.tensor_tensor(w4g[:, :], w4g[:, :], srep[:, :], ALU.mult)

                @ch.step("sync")
                def _(sync, sem=None):
                    sync.dma_start(w4_scr[128 * g:128 * (g + 1), :], w4g[:, :])
            mkg()

        for oc_ in range(32):
            def mko(oc=oc_):
                @ch.step("sync")
                def _(sync, sem=None):
                    sync.dma_start(
                        ap(w4c, 0, (w4c.shape[1], 128), (128, 32), (1, 128)),
                        ap(w4_scr, oc * 128, (D, 128), (128 * D, 32), (1, 128)))

                @ch.step("tensor")
                def _(te, sem=None):
                    for tq in range(4):
                        for g in range(NG):
                            te.matmul(g_ps[:, 512 * tq:512 * (tq + 1)],
                                      w4c[:, 128 * g:128 * (g + 1)],
                                      xsb[g][:, 512 * tq:512 * (tq + 1)],
                                      start=(g == 0), stop=False,
                                      skip_group_check=True)
                        te.matmul(g_ps[:, 512 * tq:512 * (tq + 1)],
                                  zsn[0:32, 128 * oc:128 * (oc + 1)],
                                  xs_sb[0:32, 512 * tq:512 * (tq + 1)],
                                  start=False, stop=True, skip_group_check=True)

                @ch.step("scalar")
                def _(s, sem=None):
                    s.activation(w4g[:, 0:T], g_ps[:, :], AF.Silu, bias=bsf[:, oc:oc + 1])

                @ch.step("sync")
                def _(sync, sem=None):
                    sync.dma_start(g0T[128 * oc:128 * (oc + 1), :], w4g[:, 0:T])
            mko()
        ch.emit()
    return nc


def build_combine(outf32=True):
    nc = bass.Bass("TRN2", target_bir_lowering=False)
    x = nc.dram_tensor("x", [T, D], F32, kind="ExternalInput")
    s = nc.dram_tensor("s", [T, D], BF16, kind="ExternalInput")
    g = nc.dram_tensor("g", [T, D], BF16, kind="ExternalInput")
    x1 = nc.dram_tensor("x1", [T, D], F32, kind="ExternalOutput")
    with ExitStack() as st:
        sb = lambda n, sh, dt: st.enter_context(nc.sbuf_tensor(n, sh, dt))
        xt = sb("xt", [128, D], F32)
        st_ = sb("st", [128, D], BF16)
        gt = sb("gt", [128, D], BF16)
        pt = sb("pt", [128, D], F32)
        ch = Chain(nc)
        for j_ in range(NT):
            def mk(j=j_):
                @ch.step("sync")
                def _(sync, sem=None):
                    sync.dma_start(xt[:, :], x[128 * j:128 * (j + 1), :])
                    sync.dma_start(st_[:, :], s[128 * j:128 * (j + 1), :])
                    sync.dma_start(gt[:, :], g[128 * j:128 * (j + 1), :])

                @ch.step("vector")
                def _(v, sem=None):
                    v.tensor_tensor(pt[:, :], st_[:, :], gt[:, :], ALU.mult)
                    v.tensor_tensor(xt[:, :], xt[:, :], pt[:, :], ALU.add)

                @ch.step("sync")
                def _(sync, sem=None):
                    sync.dma_start(x1[128 * j:128 * (j + 1), :], xt[:, :])
            mk()
        ch.emit()
    return nc


def build_qkv():
    nc = bass.Bass("TRN2", target_bir_lowering=False)
    xnT = nc.dram_tensor("xnT", [D, T], BF16, kind="ExternalInput")
    Wq = nc.dram_tensor("Wq", [D, D], F32, kind="ExternalInput")
    Wkv = nc.dram_tensor("Wkv", [D, 256], F32, kind="ExternalInput")
    QT = nc.dram_tensor("QT", [D, T], BF16, kind="ExternalOutput")
    KVT = nc.dram_tensor("KVT", [256, T], BF16, kind="ExternalOutput")
    NV = 32
    with ExitStack() as st:
        sb = lambda n, sh, dt: st.enter_context(nc.sbuf_tensor(n, sh, dt))
        xsb = [sb(f"xsb{v}", [128, T], BF16) for v in range(NV)]
        wc = sb("wc", [128, D], BF16)
        qout = sb("qout", [128, T], BF16)
        q_ps = st.enter_context(nc.psum_tensor("q_ps", [128, T], F32))
        ch = Chain(nc)

        @ch.step("sync")
        def _(sync, sem=None):
            for v in range(NV):
                sync.dma_start(xsb[v][:, :], xnT[128 * v:128 * (v + 1), :])

        for oc_ in range(34):
            def mko(oc=oc_):
                @ch.step("gpsimd")
                def _(gp, sem=None):
                    if oc < 32:
                        gp.dma_start(ap(wc, 0, (wc.shape[1], 128), (128, 32), (1, 128)),
                                     ap(Wq, oc * 128, (D, 128), (128 * D, 32), (1, 128)))
                    else:
                        o2 = oc - 32
                        gp.dma_start(ap(wc, 0, (wc.shape[1], 128), (128, 32), (1, 128)),
                                     ap(Wkv, o2 * 128, (256, 128), (128 * 256, 32), (1, 128)))

                @ch.step("tensor")
                def _(te, sem=None):
                    for tq in range(4):
                        for g in range(NV):
                            te.matmul(q_ps[:, 512 * tq:512 * (tq + 1)],
                                      wc[:, 128 * g:128 * (g + 1)],
                                      xsb[g][:, 512 * tq:512 * (tq + 1)],
                                      start=(g == 0), stop=(g == NV - 1),
                                      skip_group_check=True)

                @ch.step("vector")
                def _(v, sem=None):
                    v.tensor_copy(qout[:, :], q_ps[:, :])

                @ch.step("sync")
                def _(sync, sem=None):
                    if oc < 32:
                        sync.dma_start(QT[128 * oc:128 * (oc + 1), :], qout[:, :])
                    else:
                        o2 = oc - 32
                        sync.dma_start(KVT[128 * o2:128 * (o2 + 1), :], qout[:, :])
            mko()
        ch.emit()
    return nc

ATT_SCALE = float(128 ** -0.5)
WTHRESH = 60.0


def _emit_attn(ch, st, nc, heads, causal, qg_off, QT, KVT, Vtok, ONUM, ODEN,
               slopes_np, slopes, tag):
    sb = lambda n, sh, dt: st.enter_context(nc.sbuf_tensor(n + tag, sh, dt))
    qsb = [sb(f"qsb{h}", [128, T], BF16) for h in range(heads)]
    ksb = sb("ksb", [128, T], BF16)
    vtk = [sb(f"vtk{kt}", [128, 128], BF16) for kt in range(16)]
    slrep = sb("slrep", [128, heads], F32)
    iota = sb("iota", [128, 1], F32)
    onescol = sb("onescol", [128, 1], BF16)
    onesrow = sb("onesrow", [1, 128], F32)
    bias_v = sb("bias_v", [128, 1], F32)
    probs = sb("probs", [128, 512], BF16)
    msc = sb("msc", [128, 512], F32)
    maskfull = [sb(f"maskfull{i}", [128, 512], F32) for i in range(4)]
    av_sb = sb("av_sb", [128, 512], F32)
    drep = sb("drep", [128, 512], F32)
    rden = sb("rden", [1, 512], F32)
    onum_bf = sb("onum_bf", [128, 512], BF16)
    den_sb = sb("den_sb", [1, 512], F32)
    zrow = sb("zrow", [1, 512], F32)
    ztile = sb("ztile", [128, 512], BF16)
    sc_ps = st.enter_context(nc.psum_tensor("sc_ps" + tag, [128, 512], F32))
    av_ps = st.enter_context(nc.psum_tensor("av_ps" + tag, [128, 512], F32))
    den_ps = st.enter_context(nc.psum_tensor("den_ps" + tag, [1, 512], F32))
    dr_ps = st.enter_context(nc.psum_tensor("dr_ps" + tag, [128, 512], F32))

    iota_d = nc.inline_tensor(np.arange(128, dtype=np.float32)[:, None], "iota_d" + tag)
    ones_d = nc.inline_tensor(np.asarray(np.ones((128, 1), dtype=BF)), "ones_d" + tag)
    onesr_d = nc.inline_tensor(np.ones((1, 128), dtype=np.float32), "onesr_d" + tag)
    mf_np = []
    for d0 in (0, 128, 256, 384):
        p = np.arange(128)[:, None]
        c = np.arange(512)[None, :]
        keep = (c >= d0 + 128) | ((c >= d0) & (p <= c - d0))
        mf_np.append(np.where(keep, 0.0, -1e30).astype(np.float32))
    mf_d = [nc.inline_tensor(m, f"mf_d{i}" + tag) for i, m in enumerate(mf_np)]

    @ch.step("sync")
    def _(sync, sem=None):
        for h in range(heads):
            sync.dma_start(qsb[h][:, :], QT[128 * h:128 * (h + 1), :])
        sync.dma_start(ksb[:, :], KVT[0:128, :])
        for kt in range(16):
            sync.dma_start(vtk[kt][:, :], Vtok[128 * kt:128 * (kt + 1), :])
        sync.dma_start(slrep[:, :], bass.AP(slopes, 0, [[0, 128], [1, heads]]))
        sync.dma_start(iota[:, :], iota_d[:, :])
        sync.dma_start(onescol[:, :], ones_d[:, :])
        sync.dma_start(onesrow[:, :], onesr_d[:, :])
        for i in range(4):
            sync.dma_start(maskfull[i][:, :], mf_d[i][:, :])

    @ch.step("vector")
    def _(v, sem=None):
        v.memset(zrow[:, :], 0.0)
        v.memset(ztile[:, :], 0.0)

    for h_ in range(heads):
        for qc_ in range(4):
            def mk(h=h_, qc=qc_):
                q0 = qc * 512
                qglob0 = qg_off + q0
                sl = float(slopes_np[h])
                ebw = 512 if sl * 511.0 <= WTHRESH else 64
                nex = 512 // ebw
                kts = []
                for kt in range(16):
                    kb = kt * 128
                    if causal and kb > qglob0 + 511:
                        continue
                    if sl * (qglob0 - (kb + 127)) > WTHRESH:
                        continue
                    kts.append(kt)
                if not kts:
                    @ch.step("sync")
                    def _(sync, sem=None):
                        sync.dma_start(ONUM[128 * h:128 * (h + 1), q0:q0 + 512],
                                       ztile[:, :])
                        sync.dma_start(ODEN[h:h + 1, q0:q0 + 512], zrow[0:1, :])
                    return

                for ki, kt in enumerate(kts):
                    kb = kt * 128

                    def mk_kt(h=h, q0=q0, qglob0=qglob0, kt=kt, kb=kb,
                              first=(ki == 0), sl=sl, ebw=ebw, nex=nex):
                        diag = causal and (kb >= qglob0)
                        d0 = kb - qglob0 if diag else None

                        @ch.step("tensor")
                        def _(te, sem=None):
                            te.matmul(sc_ps[:, :], ksb[:, kb:kb + 128],
                                      qsb[h][:, q0:q0 + 512],
                                      start=True, stop=True, skip_group_check=True)

                        if diag:
                            @ch.step("vector")
                            def _(v, sem=None):
                                v.tensor_tensor(msc[:, :], sc_ps[:, :],
                                                maskfull[d0 // 128][:, :], ALU.add)
                        src_is_m = diag

                        for ce in range(nex):
                            c = ce * ebw

                            def mk_e(c=c, h=h, kb=kb, qglob0=qglob0, ebw=ebw,
                                     src_is_m=src_is_m):
                                @ch.step("vector")
                                def _(v, sem=None):
                                    v.tensor_scalar(bias_v[:, :], iota[:, :],
                                                    float(kb - qglob0 - c),
                                                    slrep[:, h:h + 1],
                                                    ALU.add, ALU.mult)

                                @ch.step("scalar")
                                def _(s, sem=None):
                                    srcbuf = msc if src_is_m else None
                                    if srcbuf is not None:
                                        s.activation(probs[:, c:c + ebw],
                                                     msc[:, c:c + ebw], AF.Exp,
                                                     bias=bias_v[:, 0:1],
                                                     scale=ATT_SCALE)
                                    else:
                                        s.activation(probs[:, c:c + ebw],
                                                     sc_ps[:, c:c + ebw], AF.Exp,
                                                     bias=bias_v[:, 0:1],
                                                     scale=ATT_SCALE)
                            mk_e()

                        @ch.step("tensor")
                        def _(te, sem=None):
                            te.matmul(av_ps[:, :], vtk[kt][:, :], probs[:, :],
                                      start=first, stop=False, skip_group_check=True)
                            te.matmul(den_ps[0:1, :], onescol[:, :], probs[:, :],
                                      start=first, stop=False, skip_group_check=True)
                    mk_kt()

                @ch.step("vector")
                def _(v, sem=None):
                    v.tensor_copy(den_sb[0:1, :], den_ps[0:1, :])
                    v.tensor_copy(onum_bf[:, :], av_ps[:, :])

                @ch.step("sync")
                def _(sync, sem=None):
                    sync.dma_start(ONUM[128 * h:128 * (h + 1), q0:q0 + 512],
                                   onum_bf[:, :])
                    sync.dma_start(ODEN[h:h + 1, q0:q0 + 512], den_sb[0:1, :])
            mk()


def build_attn(heads, causal, qg_off, slopes_np, tag):
    nc = bass.Bass("TRN2", target_bir_lowering=False)
    QT = nc.dram_tensor("QT", [heads * 128, T], BF16, kind="ExternalInput")
    KVT = nc.dram_tensor("KVT", [256, T], BF16, kind="ExternalInput")
    Vtok = nc.dram_tensor("Vtok", [T, 128], BF16, kind="ExternalInput")
    slopes = nc.dram_tensor("slopes", [1, heads], F32, kind="ExternalInput")
    ONUM = nc.dram_tensor("ONUM", [heads * 128, T], BF16, kind="ExternalOutput")
    ODEN = nc.dram_tensor("ODEN", [heads, T], F32, kind="ExternalOutput")
    with ExitStack() as st:
        ch = Chain(nc)
        _emit_attn(ch, st, nc, heads, causal, qg_off, QT, KVT, Vtok, ONUM, ODEN,
                   slopes_np, slopes, tag)
        ch.emit()
    return nc

def build_attn_comb():
    nc = bass.Bass("TRN2", target_bir_lowering=False)
    Tnum = nc.dram_tensor("Tnum", [D, T], BF16, kind="ExternalInput")
    Tden = nc.dram_tensor("Tden", [32, T], F32, kind="ExternalInput")
    O1num = nc.dram_tensor("O1num", [16 * 128, T], BF16, kind="ExternalInput")
    O1den = nc.dram_tensor("O1den", [16, T], F32, kind="ExternalInput")
    O2num = nc.dram_tensor("O2num", [16 * 128, T], BF16, kind="ExternalInput")
    O2den = nc.dram_tensor("O2den", [16, T], F32, kind="ExternalInput")
    m1 = nc.dram_tensor("m1", [1, 1], F32, kind="ExternalInput")
    m2 = nc.dram_tensor("m2", [1, 1], F32, kind="ExternalInput")
    ATTNT = nc.dram_tensor("ATTNT", [D, T], BF16, kind="ExternalOutput")
    with ExitStack() as st:
        sb = lambda n, sh, dt: st.enter_context(nc.sbuf_tensor(n, sh, dt))
        tnum = sb("tnum", [128, T], BF16)
        onum = sb("onum", [128, T], BF16)
        trow = sb("trow", [1, T], F32)
        orow = sb("orow", [1, T], F32)
        drow = sb("drow", [1, T], F32)
        rrow = sb("rrow", [1, T], F32)
        m1rep = sb("m1rep", [128, 1], F32)
        m2rep = sb("m2rep", [128, 1], F32)
        onesrow = sb("onesrow", [1, 128], F32)
        numf = sb("numf", [128, T], F32)
        outbf = sb("outbf", [128, T], BF16)
        drep = sb("drep", [128, T], F32)
        dr_ps = st.enter_context(nc.psum_tensor("dr_ps", [128, T], F32))
        onesr_d = nc.inline_tensor(np.ones((1, 128), dtype=np.float32), "onesr_d")
        ch = Chain(nc)

        @ch.step("sync")
        def _(sync, sem=None):
            sync.dma_start(m1rep[:, :], bass.AP(m1, 0, [[0, 128], [1, 1]]))
            sync.dma_start(m2rep[:, :], bass.AP(m2, 0, [[0, 128], [1, 1]]))
            sync.dma_start(onesrow[:, :], onesr_d[:, :])

        for h_ in range(32):
            def mk(h=h_):
                even = (h % 2 == 0)
                Od, On = (O1den, O1num) if even else (O2den, O2num)
                orow_idx = h // 2
                mrep = m1rep if even else m2rep

                @ch.step("sync")
                def _(sync, sem=None):
                    sync.dma_start(trow[0:1, :], Tden[h:h + 1, :])
                    sync.dma_start(orow[0:1, :], Od[orow_idx:orow_idx + 1, :])
                    sync.dma_start(tnum[:, :], Tnum[128 * h:128 * (h + 1), :])
                    sync.dma_start(onum[:, :],
                                   On[128 * orow_idx:128 * (orow_idx + 1), :])

                @ch.step("vector")
                def _(v, sem=None):
                    v.scalar_tensor_tensor(drow[0:1, :], orow[0:1, :], mrep[0:1, 0:1],
                                           trow[0:1, :], ALU.mult, ALU.add)
                    v.reciprocal(rrow[0:1, :], drow[0:1, :])
                    v.scalar_tensor_tensor(numf[:, :], onum[:, :], mrep[:, 0:1],
                                           tnum[:, :], ALU.mult, ALU.add)

                @ch.step("tensor")
                def _(te, sem=None):
                    for c in range(0, T, 512):
                        te.matmul(dr_ps[:, c:c + 512], onesrow[:, :],
                                  rrow[0:1, c:c + 512], start=True, stop=True,
                                  skip_group_check=True)

                @ch.step("vector")
                def _(v, sem=None):
                    v.tensor_copy(drep[:, :], dr_ps[:, :])
                    v.tensor_tensor(outbf[:, :], numf[:, :], drep[:, :], ALU.mult)

                @ch.step("sync")
                def _(sync, sem=None):
                    sync.dma_start(ATTNT[128 * h:128 * (h + 1), :], outbf[:, :])
            mk()
        ch.emit()
    return nc


def build_wo():
    nc = bass.Bass("TRN2", target_bir_lowering=False)
    aT = nc.dram_tensor("aT", [D, T], BF16, kind="ExternalInput")
    Wo = nc.dram_tensor("Wo", [D, D], F32, kind="ExternalInput")
    s1T = nc.dram_tensor("s1T", [D, T], BF16, kind="ExternalOutput")
    NV = 32
    with ExitStack() as st:
        sb = lambda n, sh, dt: st.enter_context(nc.sbuf_tensor(n, sh, dt))
        xsb = [sb(f"xsb{v}", [128, T], BF16) for v in range(NV)]
        wc = sb("wc", [128, D], BF16)
        qout = sb("qout", [128, T], BF16)
        q_ps = st.enter_context(nc.psum_tensor("q_ps", [128, T], F32))
        ch = Chain(nc)

        @ch.step("sync")
        def _(sync, sem=None):
            for v in range(NV):
                sync.dma_start(xsb[v][:, :], aT[128 * v:128 * (v + 1), :])

        for oc_ in range(32):
            def mko(oc=oc_):
                @ch.step("gpsimd")
                def _(gp, sem=None):
                    gp.dma_start(ap(wc, 0, (wc.shape[1], 128), (128, 32), (1, 128)),
                                 ap(Wo, oc * 128, (D, 128), (128 * D, 32), (1, 128)))

                @ch.step("tensor")
                def _(te, sem=None):
                    for tq in range(4):
                        for g in range(NV):
                            te.matmul(q_ps[:, 512 * tq:512 * (tq + 1)],
                                      wc[:, 128 * g:128 * (g + 1)],
                                      xsb[g][:, 512 * tq:512 * (tq + 1)],
                                      start=(g == 0), stop=(g == NV - 1),
                                      skip_group_check=True)

                @ch.step("vector")
                def _(v, sem=None):
                    v.tensor_copy(qout[:, :], q_ps[:, :])

                @ch.step("sync")
                def _(sync, sem=None):
                    sync.dma_start(s1T[128 * oc:128 * (oc + 1), :], qout[:, :])
            mko()
        ch.emit()
    return nc


def build_ffn1():
    nc = bass.Bass("TRN2", target_bir_lowering=False)
    xnT = nc.dram_tensor("xnT", [D, T], BF16, kind="ExternalInput")
    WT = nc.dram_tensor("WT", [D, D], F32, kind="ExternalInput")
    biasc = nc.dram_tensor("biasc", [128, 32], F32, kind="ExternalInput")
    AT = nc.dram_tensor("AT", [D, 48], F32, kind="ExternalInput")
    BS = nc.dram_tensor("BS", [48, D], F32, kind="ExternalInput")
    lsi = nc.dram_tensor("lsi", [1, 4], I32, kind="ExternalInput")
    bval = nc.dram_tensor("bval", [1, 1], F32, kind="ExternalInput")
    arow = nc.dram_tensor("arow", [48, 1], F32, kind="ExternalInput")
    g1T = nc.dram_tensor("g1T", [D, T], BF16, kind="ExternalOutput")
    NV = 32
    with ExitStack() as st:
        sb = lambda n, sh, dt: st.enter_context(nc.sbuf_tensor(n, sh, dt))
        xsb = [sb(f"xsb{v}", [128, T], BF16) for v in range(NV)]
        wc = sb("wc", [128, D], BF16)
        atsb = sb("atsb", [128, 48 * 32], BF16)
        bssb = sb("bssb", [48, D], BF16)
        bsf = sb("bsf", [128, 32], F32)
        lsirep = sb("lsirep", [48, 4], I32)
        lsif = sb("lsif", [48, 4], F32)
        brep = sb("brep", [48, 1], F32)
        arsb = sb("arsb", [48, 1], F32)
        cnt = sb("cnt", [48, 1], F32)
        seg = sb("seg", [48, 1], F32)
        use = sb("use", [48, 1], F32)
        aidx = sb("aidx", [48, 1], F32)
        msk = sb("msk", [48, 1], F32)
        rT = sb("rT", [48, T], BF16)
        gout = sb("gout", [128, T], BF16)
        r_ps = st.enter_context(nc.psum_tensor("r_ps", [48, T], F32))
        g_ps = st.enter_context(nc.psum_tensor("g_ps", [128, T], F32))
        ch = Chain(nc)

        @ch.step("sync")
        def _(sync, sem=None):
            for v in range(NV):
                sync.dma_start(xsb[v][:, :], xnT[128 * v:128 * (v + 1), :])
            sync.dma_start(lsirep[:, :], bass.AP(lsi, 0, [[0, 48], [1, 4]]))
            sync.dma_start(brep[:, :], bass.AP(bval, 0, [[0, 48], [1, 1]]))
            sync.dma_start(arsb[:, :], arow[:, :])
            sync.dma_start(bsf[:, :], biasc[:, :])

        @ch.step("gpsimd")
        def _(gp, sem=None):
            for v in range(NV):
                gp.dma_start(ap(atsb, v * 48, (atsb.shape[1], 128), (1, 48)),
                             AT[128 * v:128 * (v + 1), :])
            gp.dma_start(bssb[0:48, :], BS[:, :])

        @ch.step("vector")
        def _(v, sem=None):
            v.tensor_copy(lsif[0:48, :], lsirep[0:48, :])
            v.tensor_scalar(lsif[0:48, :], lsif[0:48, :], brep[0:48, 0:1], None,
                            ALU.is_le)
            v.tensor_reduce(cnt[0:48, :], lsif[0:48, :], mybir.AxisListType.X,
                            ALU.add)
            v.tensor_scalar(seg[0:48, :], cnt[0:48, :], -1.0, None, ALU.add)
            v.tensor_scalar(use[0:48, :], seg[0:48, :], 0.0, None, ALU.is_gt)
            v.tensor_scalar(aidx[0:48, :], seg[0:48, :], -1.0, 0.0, ALU.add, ALU.max)
            v.tensor_scalar(aidx[0:48, :], aidx[0:48, :], 2.0, None, ALU.min)
            v.tensor_tensor(msk[0:48, :], aidx[0:48, :], arsb[0:48, :], ALU.is_equal)
            v.tensor_tensor(msk[0:48, :], msk[0:48, :], use[0:48, :], ALU.mult)

        @ch.step("tensor")
        def _(te, sem=None):
            for tq in range(4):
                for v in range(NV):
                    te.matmul(r_ps[0:48, 512 * tq:512 * (tq + 1)],
                              ap(atsb, v * 48, (atsb.shape[1], 128), (1, 48)),
                              xsb[v][:, 512 * tq:512 * (tq + 1)],
                              start=(v == 0), stop=(v == NV - 1),
                              skip_group_check=True)

        @ch.step("vector")
        def _(v, sem=None):
            v.tensor_scalar(rT[0:48, :], r_ps[0:48, :], msk[0:48, 0:1], None,
                            ALU.mult)

        for oc_ in range(32):
            def mko(oc=oc_):
                @ch.step("gpsimd")
                def _(gp, sem=None):
                    gp.dma_start(ap(wc, 0, (wc.shape[1], 128), (128, 32), (1, 128)),
                                 ap(WT, oc * 128, (D, 128), (128 * D, 32), (1, 128)))

                @ch.step("tensor")
                def _(te, sem=None):
                    for tq in range(4):
                        for g in range(NV):
                            te.matmul(g_ps[:, 512 * tq:512 * (tq + 1)],
                                      wc[:, 128 * g:128 * (g + 1)],
                                      xsb[g][:, 512 * tq:512 * (tq + 1)],
                                      start=(g == 0), stop=False,
                                      skip_group_check=True)
                        te.matmul(g_ps[:, 512 * tq:512 * (tq + 1)],
                                  bssb[0:48, 128 * oc:128 * (oc + 1)],
                                  rT[0:48, 512 * tq:512 * (tq + 1)],
                                  start=False, stop=True, skip_group_check=True)

                @ch.step("scalar")
                def _(s, sem=None):
                    s.activation(gout[:, :], g_ps[:, :], AF.Silu,
                                 bias=bsf[:, oc:oc + 1])

                @ch.step("sync")
                def _(sync, sem=None):
                    sync.dma_start(g1T[128 * oc:128 * (oc + 1), :], gout[:, :])
            mko()
        ch.emit()
    return nc


_PROG_CACHE = {}


def _prog(name, fn, *a):
    key = (name,) + a
    if key not in _PROG_CACHE:
        _PROG_CACHE[key] = fn(*a)
    return _PROG_CACHE[key]


def _run(nc, in_maps):
    res = run_bass_kernel_spmd(nc, in_maps, core_ids=list(range(NCORE)))
    return res.results


def kernel(**inp):
    C = np.ascontiguousarray
    x = inp["L0_attn_x"]
    xs = [C(x[c // 2, (c % 2) * T:(c % 2 + 1) * T]) for c in range(NCORE)]
    scan_layout = lambda a: C(np.asarray(a, np.float32).reshape(4, 128).T)

    r = _run(_prog("ln", build_ln), [{"x": s} for s in xs])
    xn = [r[c]["xn"] for c in range(NCORE)]
    xnT = [C(np.asarray(a).T) for a in xn]

    WinT = C(inp["L0_attn_Win"].T)
    r = _run(_prog("h0", build_h0),
             [{"xnT": xnT[c], "WinT": WinT, "bin": inp["L0_attn_bin"][None]}
              for c in range(NCORE)])
    h = [r[c]["h0"] for c in range(NCORE)]

    shamt = ((np.arange(128) % 8) * 4).astype(np.int32)[:, None]
    biasc0 = C(inp["L0_ffn_bias"].reshape(32, 128).T)
    r = _run(_prog("gate0", build_gate0),
             [{"xnT": xnT[c], "qw": inp["L0_ffn_qweight"], "qz": inp["L0_ffn_qzeros"],
               "scales": inp["L0_ffn_scales"], "biasc": biasc0, "shamt": shamt}
              for c in range(NCORE)])
    g0 = [C(np.asarray(r[c]["g0T"]).T) for c in range(NCORE)]

    nc_a = _prog("mini_a", build_mini_a)
    nc_b = _prog("mini_b", build_mini_b)
    for dd in range(4):
        phzs = np.concatenate([scan_layout(inp["L0_attn_phz_re"][dd]),
                               scan_layout(inp["L0_attn_phz_im"][dd])], 1)
        phzrow = np.stack([inp["L0_attn_phz_re"][dd],
                           inp["L0_attn_phz_im"][dd]]).astype(np.float32)
        cinitA = np.concatenate([scan_layout(inp["L0_attn_lc_re"][dd]),
                                 scan_layout(inp["L0_attn_lc_im"][dd])], 1)
        cz = np.zeros((128, 8), np.float32)
        ra = _run(nc_a, [{"h": h[c], "lnw": inp["L0_attn_ln1_w"][dd][None],
                          "lnb": inp["L0_attn_ln1_b"][dd][None],
                          "scl": inp["L0_attn_sc_lin"][dd][None],
                          "phzs": phzs, "phzrow": phzrow,
                          "cinit": cinitA if c % 2 == 0 else cz}
                         for c in range(NCORE)])
        W1T = C(inp["L0_attn_W1"][dd].T)
        W2T = C(inp["L0_attn_W2"][dd].T)
        b1cv = C(inp["L0_attn_b1"][dd].reshape(16, 128).T)
        rb = _run(nc_b, [{"hmid": ra[c]["hmid"],
                          "cextra": cz if c % 2 == 0 else np.asarray(ra[c - 1]["cfin"]),
                          "Pp1s": np.asarray(ra[c]["Pp1o"]),
                          "P128i": np.asarray(ra[c]["P128o"]),
                          "ln2w": inp["L0_attn_ln2_w"][dd][None],
                          "ln2b": inp["L0_attn_ln2_b"][dd][None],
                          "W1T": W1T, "b1c": b1cv, "W2T": W2T,
                          "b2": inp["L0_attn_b2"][dd][None]}
                         for c in range(NCORE)])
        h = [rb[c]["hpost"] for c in range(NCORE)]

    WoutT = C(inp["L0_attn_Wout"].T)
    r = _run(_prog("sfinal", build_sfinal),
             [{"h": h[c], "lnw": inp["L0_attn_lnl_w"][None],
               "lnb": inp["L0_attn_lnl_b"][None], "WoutT": WoutT,
               "bout": inp["L0_attn_bout"][None]} for c in range(NCORE)])
    s0 = [r[c]["s0"] for c in range(NCORE)]

    r = _run(_prog("comb", build_combine),
             [{"x": xs[c], "s": s0[c], "g": g0[c]} for c in range(NCORE)])
    x1 = [r[c]["x1"] for c in range(NCORE)]

    r = _run(_prog("ln", build_ln), [{"x": x1[c]} for c in range(NCORE)])
    xn1T = [C(np.asarray(r[c]["xn"]).T) for c in range(NCORE)]

    r = _run(_prog("qkv", build_qkv),
             [{"xnT": xn1T[c], "Wq": inp["L1_attn_Wq"], "Wkv": inp["L1_attn_Wkv"]}
              for c in range(NCORE)])
    QT = [np.asarray(r[c]["QT"]) for c in range(NCORE)]
    KVT = [np.asarray(r[c]["KVT"]) for c in range(NCORE)]

    sl = alibi_slopes(32)
    evens, odds = list(range(0, 32, 2)), list(range(1, 32, 2))
    Vtok = [C(KVT[c][128:256].T) for c in range(NCORE)]
    rt = _run(_prog("attri", build_attn, 32, True, 0, tuple(map(float, sl)), "tri"),
              [{"QT": QT[c], "KVT": KVT[c], "Vtok": Vtok[c], "slopes": sl[None]}
               for c in range(NCORE)])
    ins_off = []
    slw = tuple(float(v) for v in sl[odds])
    for c in range(NCORE):
        b = c // 2
        A, Bc = 2 * b, 2 * b + 1
        hl = evens if c % 2 == 0 else odds
        qoff = C(np.concatenate([QT[Bc][128 * hh:128 * (hh + 1)] for hh in hl], 0))
        ins_off.append({"QT": qoff, "KVT": KVT[A], "Vtok": Vtok[A],
                        "slopes": sl[hl][None]})
    ro = _run(_prog("attoff", build_attn, 16, False, 2048, slw, "off"), ins_off)
    ins_c = []
    z1 = np.zeros((1, 1), np.float32)
    o1 = np.ones((1, 1), np.float32)
    for c in range(NCORE):
        b = c // 2
        A = 2 * b
        if c % 2 == 0:
            ins_c.append({"Tnum": np.asarray(rt[c]["ONUM"]),
                          "Tden": np.asarray(rt[c]["ODEN"]),
                          "O1num": np.asarray(ro[c]["ONUM"]),
                          "O1den": np.asarray(ro[c]["ODEN"]),
                          "O2num": np.asarray(ro[c]["ONUM"]),
                          "O2den": np.asarray(ro[c]["ODEN"]),
                          "m1": z1, "m2": z1})
        else:
            ins_c.append({"Tnum": np.asarray(rt[c]["ONUM"]),
                          "Tden": np.asarray(rt[c]["ODEN"]),
                          "O1num": np.asarray(ro[A]["ONUM"]),
                          "O1den": np.asarray(ro[A]["ODEN"]),
                          "O2num": np.asarray(ro[c]["ONUM"]),
                          "O2den": np.asarray(ro[c]["ODEN"]),
                          "m1": o1, "m2": o1})
    r = _run(_prog("attcomb", build_attn_comb), ins_c)
    ATTNT = [np.asarray(r[c]["ATTNT"]) for c in range(NCORE)]

    r = _run(_prog("wo", build_wo),
             [{"aT": ATTNT[c], "Wo": inp["L1_attn_Wo"]} for c in range(NCORE)])
    s1 = [C(np.asarray(r[c]["s1T"]).T) for c in range(NCORE)]

    WT = C(inp["L1_ffn_weight"].T)
    biasc1 = C(inp["L1_ffn_bias"].reshape(32, 128).T)
    AT = C(inp["L1_ffn_lora_A"].transpose(2, 0, 1).reshape(D, 48))
    BS = C(inp["L1_ffn_lora_B"].transpose(0, 2, 1).reshape(48, D))
    arow = np.repeat(np.arange(3, dtype=np.float32), 16)[:, None]
    lsi = np.asarray(inp["L1_ffn_lora_start_indices"], np.int32)[None]
    r = _run(_prog("ffn1", build_ffn1),
             [{"xnT": xn1T[c], "WT": WT, "biasc": biasc1, "AT": AT, "BS": BS,
               "lsi": lsi, "bval": np.full((1, 1), float(c // 2), np.float32),
               "arow": arow} for c in range(NCORE)])
    g1 = [C(np.asarray(r[c]["g1T"]).T) for c in range(NCORE)]

    r = _run(_prog("comb", build_combine),
             [{"x": x1[c], "s": s1[c], "g": g1[c]} for c in range(NCORE)])
    xo = [r[c]["x1"] for c in range(NCORE)]

    out = np.empty((4, 4096, D), np.float32)
    for c in range(NCORE):
        out[c // 2, (c % 2) * T:(c % 2 + 1) * T] = np.asarray(xo[c], np.float32)
    return out
